# version 43
# baseline (speedup 1.0000x reference)
import numpy as np
import ml_dtypes
from contextlib import ExitStack

import concourse.bass as bass
import concourse.mybir as mybir
from concourse.bass_utils import run_bass_kernel_spmd

F32 = mybir.dt.float32
F32R = mybir.dt.float32r
BF16 = mybir.dt.bfloat16
AF = mybir.ActivationFunctionType
ALU = mybir.AluOpType
AX = mybir.AxisListType.X

B, N, D = 8, 4096, 1024
T = 512
KC = D // 128
NT = N // T
HS = 2048
NE = 16
H = 256
TAPS = 20
HALO = 32
CW = T + HALO
BIG = 1.0e30

ENGINES = ("sync", "tensor", "vector", "scalar", "gpsimd")
ENG_SEM = {"tensor": "pe", "vector": "ve", "scalar": "se", "gpsimd": "po"}
DMA_SEMS = ("d_x0", "d_x1", "d_sh0", "d_sh1", "d_sh2", "d_ex0", "d_ex1", "d_dnA0", "d_dnA1", "d_dnC0", "d_dnC1", "d_dnB0", "d_dnB1",
             "d_out0", "d_out1", "d_const") + tuple(f"d_c{i}" for i in range(16))


class Build:

    def __init__(self):
        self.ops = {e: [] for e in ENGINES}
        self.sems = {}
        self.counts = {}
        for name in list(ENG_SEM.values()) + list(DMA_SEMS):
            self.counts[name] = 0
        self.last_wait = {e: {} for e in ENGINES}
        self.res = {}

    def _wait(self, eng, sem_name, val):
        if val <= 0:
            return
        lw = self.last_wait[eng].get(sem_name, 0)
        if val <= lw:
            return
        self.last_wait[eng][sem_name] = val
        self.ops[eng].append(("wait", sem_name, val))

    def emit(self, eng, fns, reads=(), writes=(), dma=None):
        if not isinstance(fns, (list, tuple)):
            fns = [fns]
        if dma is not None:
            self._wait(eng, dma, self.counts[dma])
        for r in reads:
            rec = self.res.get(r)
            if rec and rec.get("w"):
                s, v = rec["w"]
                self._wait(eng, s, v)
        for r in writes:
            rec = self.res.get(r)
            if rec:
                if rec.get("w"):
                    s, v = rec["w"]
                    self._wait(eng, s, v)
                for (s, v) in rec.get("r", []):
                    self._wait(eng, s, v)
        if dma is not None:
            self.counts[dma] += 16 * len(fns)
            val = self.counts[dma]
            self.ops[eng].append(("dma", fns, dma))
            ev = (dma, val)
        else:
            sem_name = ENG_SEM[eng]
            self.counts[sem_name] += 1
            val = self.counts[sem_name]
            self.ops[eng].append(("ops", fns, sem_name))
            ev = (sem_name, val)
        for r in reads:
            rec = self.res.setdefault(r, {"w": None, "r": []})
            rec["r"].append(ev)
        for r in writes:
            rec = self.res.setdefault(r, {"w": None, "r": []})
            rec["w"] = ev
            rec["r"] = []
        return val

    def finalize_waits(self, eng):
        for s, v in sorted(self.counts.items()):
            self._wait(eng, s, v)

    def replay(self, eng_name, eng):
        for item in self.ops[eng_name]:
            if item[0] == "wait":
                _, s, v = item
                eng.wait_ge(self.sems[s], v)
            elif item[0] == "dma":
                _, fns, sem_name = item
                for f in fns:
                    f(eng).then_inc(self.sems[sem_name], 16)
            else:
                _, fns, sem_name = item
                ins = None
                for f in fns:
                    ins = f(eng)
                assert ins is not None
                ins.then_inc(self.sems[sem_name], 1)


def build_program(n_tiles=NT):
    nc = bass.Bass("TRN2", target_bir_lowering=False)

    x_fm = nc.dram_tensor("x_fm", [D, N], F32, kind="ExternalInput").ap()
    shp = nc.dram_tensor("shp", [16 * 128, 2048], F32, kind="ExternalInput").ap()
    exp_p = nc.dram_tensor("exp_p", [16 * 128, 4096], F32, kind="ExternalInput").ap()
    dnp = nc.dram_tensor("dnp", [8 * 128, 8192], F32, kind="ExternalInput").ap()
    etp = nc.dram_tensor("etp", [128, 8 * 21], F32, kind="ExternalInput").ap()
    rwp = nc.dram_tensor("rwp", [128, 8 * 16], F32, kind="ExternalInput").ap()
    biasp = nc.dram_tensor("biasp", [128, 8], F32, kind="ExternalInput").ap()
    cbp = nc.dram_tensor("cbp", [1, 16], F32, kind="ExternalInput").ap()
    selp = nc.dram_tensor("selp", [16, 16 * 128], BF16, kind="ExternalInput").ap()
    identbf_in = nc.dram_tensor("identbf", [128, 128], BF16, kind="ExternalInput").ap()

    out_fm = nc.dram_tensor("out_fm", [D, N], F32, kind="ExternalOutput").ap()

    sh_sc = nc.dram_tensor("sh_sc", [16 * 128, 2048], BF16, kind="Internal").ap()
    ex_sc = nc.dram_tensor("ex_sc", [16 * 128, 4096], BF16, kind="Internal").ap()
    dn_sc = nc.dram_tensor("dn_sc", [8 * 128, 8192], BF16, kind="Internal").ap()

    bld = Build()

    with ExitStack() as ctx:
        cc = ctx.enter_context
        x_sb = [cc(nc.sbuf_tensor(f"x_sb{p}", [128, KC * T], BF16)) for p in range(2)]
        xn_bf2 = [cc(nc.sbuf_tensor(f"xn_bf{p}", [128, KC * CW], BF16)) for p in range(2)]
        h_conv = cc(nc.sbuf_tensor("h_conv", [128, KC * T], BF16))
        h_s = cc(nc.sbuf_tensor("h_s", [128, 16 * T], BF16))
        h_w = cc(nc.sbuf_tensor("h_w", [128, 32 * T], BF16))
        silu_sb = [cc(nc.sbuf_tensor(f"silu{p}", [128, T], F32)) for p in range(2)]
        u_sb = [cc(nc.sbuf_tensor(f"usb{p}", [128, T], F32)) for p in range(2)]
        xr2 = [cc(nc.sbuf_tensor(f"xr{p}", [128, T], BF16)) for p in range(2)]
        out_st = [cc(nc.sbuf_tensor(f"out_st{p}", [128, T], F32)) for p in range(2)]
        wsh = [cc(nc.sbuf_tensor(f"wsh{p}", [128, 2048], BF16)) for p in range(3)]
        wexp = [cc(nc.sbuf_tensor(f"wexp{p}", [128, 4096], BF16)) for p in range(2)]
        wdn = [cc(nc.sbuf_tensor(f"wdn{p}", [128, 8192], BF16)) for p in range(2)]
        diag = [cc(nc.sbuf_tensor(f"diag{p}", [128, TAPS * 128], BF16)) for p in range(4)]
        et_sb = cc(nc.sbuf_tensor("et_sb", [128, 8 * 21], F32))
        rw_sb = cc(nc.sbuf_tensor("rw_sb", [128, 128], F32))
        xr_r = cc(nc.sbuf_tensor("xr_r", [16, 512], F32R))
        rw_bf = cc(nc.sbuf_tensor("rw_bf", [128, 128], BF16))
        hcv2 = [cc(nc.sbuf_tensor(f"hcv{p}", [128, T], F32)) for p in range(2)]
        ident_r = cc(nc.sbuf_tensor("ident_r", [16, 16], F32R))
        bias_col = cc(nc.sbuf_tensor("bias_col", [128, 8], F32))
        cb_sb = cc(nc.sbuf_tensor("cb_sb", [1, 16], F32))
        bias16 = cc(nc.sbuf_tensor("bias16", [128, 16], F32))
        rs16 = cc(nc.sbuf_tensor("rs16", [128, 16], F32))
        sel_sb = cc(nc.sbuf_tensor("sel_sb", [16, 2048], BF16))
        identbf = cc(nc.sbuf_tensor("identbf_sb", [128, 128], BF16))
        ones_col = cc(nc.sbuf_tensor("ones_col", [128, 1], F32))
        ones_col_r = cc(nc.sbuf_tensor("ones_col_r", [128, 1], BF16))
        ones_row = cc(nc.sbuf_tensor("ones_row", [1, 128], F32))
        ab8 = cc(nc.sbuf_tensor("ab8", [128, 8], F32))
        stats = cc(nc.sbuf_tensor("stats", [1, 2048], F32))
        scores4 = cc(nc.sbuf_tensor("scores4", [128, 64], F32))
        sb64 = cc(nc.sbuf_tensor("sb64", [128, 64], F32))
        mask64 = cc(nc.sbuf_tensor("mask64", [128, 64], F32))
        pen64 = cc(nc.sbuf_tensor("pen64", [128, 64], F32))
        bias64 = cc(nc.sbuf_tensor("bias64", [128, 64], F32))
        g16a = cc(nc.sbuf_tensor("g16a", [128, 16], F32))
        g16b = cc(nc.sbuf_tensor("g16b", [128, 16], F32))
        g16c = cc(nc.sbuf_tensor("g16c", [128, 16], F32))
        g16d = cc(nc.sbuf_tensor("g16d", [128, 16], F32))
        r4a = cc(nc.sbuf_tensor("r4a", [128, 4], F32))
        r4b = cc(nc.sbuf_tensor("r4b", [128, 4], F32))
        comb_fm = cc(nc.sbuf_tensor("comb_fm", [16, T], BF16))
        comb_tm = cc(nc.sbuf_tensor("comb_tm", [128, 64], BF16))

        ps = [cc(nc.psum_tensor(f"ps{i}", [128, 512], F32)) for i in range(8)]

        for name in list(ENG_SEM.values()) + list(DMA_SEMS):
            bld.sems[name] = cc(nc.semaphore(name))

        block = cc(nc.Block())

        def xch(buf, kc):
            return buf[:, kc * T:(kc + 1) * T]

        def xnbf_tok(i, kc):
            return xn_bf2[i % 2][:, kc * CW + HALO: kc * CW + HALO + T]

        def xnbf_shift(i, kc, j):
            return xn_bf2[i % 2][:, kc * CW + HALO - j: kc * CW + HALO - j + T]

        def XNBF(i):
            return tuple(f"xnbf{i % 2}_{k}" for k in range(KC))

        m_ = stats[:, 0:512]
        q_ = stats[:, 512:1024]
        s2 = stats[:, 1024:1536]
        s3 = stats[:, 1536:2048]
        a_ = stats[:, 512:1024]
        b_ = stats[:, 1536:2048]
        s2i = stats[:, 1024:1536].bitcast(mybir.dt.int32)
        s3i = stats[:, 1536:2048].bitcast(mybir.dt.int32)

        const_pairs = [
            (et_sb[:], etp[:, :], "et_sb"),
            (rw_sb[:], rwp[:, :], "rw_sb"),
            (bias_col[:], biasp[:, :], "bias_col"),
            (cb_sb[:], cbp[:, :], "cb_sb"),
            (sel_sb[:], selp[:, :], "sel_sb"),
            (identbf[:], identbf_in[:, :], "identbf"),
        ]
        bld.emit("sync",
                 [(lambda e, d=d, s=s: e.dma_start(out=d, in_=s)) for d, s, _ in const_pairs],
                 writes=tuple(r for _, _, r in const_pairs), dma="d_const")
        bld.emit("vector", lambda e: e.memset(ones_col[:], 1.0), writes=("ones_col",))
        bld.emit("scalar", lambda e: e.copy(ones_col_r[:], ones_col[:]),
                 reads=("ones_col",), writes=("ones_col_r",))
        bld.emit("scalar", lambda e: e.copy(ident_r[:], identbf[0:16, 0:16]),
                 reads=("identbf",), writes=("ident_r",))
        bld.emit("scalar", lambda e: e.copy(rw_bf[:], rw_sb[:]),
                 reads=("rw_sb",), writes=("rw_bf",))
        bld.emit("vector", lambda e: e.memset(ones_row[:], 1.0), writes=("ones_row",))
        bld.emit("vector", lambda e: e.memset(xn_bf2[0][:], 0.0), writes=XNBF(0))
        bld.emit("vector", lambda e: e.memset(xn_bf2[1][:], 0.0), writes=XNBF(1))
        bld.emit("tensor",
                 lambda e: e.matmul(ps[6][:, 0:16], ones_row[:], cb_sb[:], start=True, stop=True),
                 reads=("ones_row", "cb_sb"), writes=("ps6",))
        bld.emit("vector", lambda e: e.tensor_copy(bias16[:], ps[6][:, 0:16]),
                 reads=("ps6",), writes=("bias16",))
        for _s in range(4):
            bld.emit("vector",
                     lambda e, _s=_s: e.tensor_copy(bias64[:, _s * 16:(_s + 1) * 16], bias16[:]),
                     reads=("bias16",), writes=("bias64",))
        bld.emit("tensor",
                 [(lambda e, kc=kc: e.matmul(ps[7][0:1, 0:16], ones_col_r[:],
                                             rw_bf[:, kc * 16:(kc + 1) * 16],
                                             start=(kc == 0), stop=(kc == KC - 1)))
                  for kc in range(KC)],
                 reads=("rw_bf", "ones_col_r"), writes=("ps7",))
        bld.emit("vector", lambda e: e.tensor_copy(stats[:, 0:16], ps[7][0:1, 0:16]),
                 reads=("ps7",), writes=("stats",))
        bld.emit("tensor",
                 lambda e: e.matmul(ps[6][:, 0:16], ones_row[:], stats[0:1, 0:16], start=True, stop=True),
                 reads=("ones_row", "stats"), writes=("ps6",))
        bld.emit("vector", lambda e: e.tensor_copy(rs16[:], ps[6][:, 0:16]),
                 reads=("ps6",), writes=("rs16",))
        def emit_diag_build(kc):
            par = kc % 4
            bld.emit("vector",
                     [(lambda e, j=j, kc=kc, par=par: e.tensor_scalar(
                         diag[par][:, (j - 1) * 128: j * 128], identbf[:],
                         et_sb[:, kc * 21 + j: kc * 21 + j + 1], None, ALU.mult))
                      for j in range(1, 17)],
                     reads=("identbf", "et_sb"), writes=(f"diag{par}",))

        def emit_x_load(i):
            par = i % 2
            t0 = i * T
            bld.emit("gpsimd",
                     [(lambda e, kc=kc, par=par, s=x_fm[kc * 128:(kc + 1) * 128, t0:t0 + T]:
                       e.dma_start(out=xch(x_sb[par], kc), in_=s)) for kc in range(KC)],
                     writes=tuple(f"x{par}_{k}" for k in range(KC)), dma=f"d_x{par}")

        def emit_square(i, kc):
            par = i % 2
            tb = xr2[kc % 2]
            bld.emit("scalar", lambda e, kc=kc, tb=tb, par=par: e.square(tb[:], xch(x_sb[par], kc)),
                     reads=(f"x{par}_{kc}",), writes=(f"xr{kc % 2}",))



        def emit_sum_mm(i):
            par = i % 2
            bld.emit("tensor",
                     [(lambda e, kc=kc, par=par: e.matmul(ps[0][0:1, :], ones_col_r[:], xch(x_sb[par], kc),
                                                          start=(kc == 0), stop=(kc == KC - 1)))
                      for kc in range(KC)],
                     reads=tuple(f"x{par}_{k}" for k in range(KC)) + ("ones_col_r",),
                     writes=("ps0",))

        def emit_sq_mm(kc):
            tb = xr2[kc % 2]
            bld.emit("tensor",
                     lambda e, kc=kc, tb=tb: e.matmul(ps[1][0:1, :], ones_col_r[:], tb[:],
                                                      start=(kc == 0), stop=(kc == KC - 1)),
                     reads=(f"xr{kc % 2}", "ones_col_r"), writes=("ps1",))

        def emit_stats_math(i):
            bld.emit("vector", lambda e: e.tensor_scalar(m_, ps[0][0:1, :], 1.0 / D, None, ALU.mult),
                     reads=("ps0",), writes=("stats",))
            bld.emit("vector", lambda e: e.tensor_scalar(q_, ps[1][0:1, :], 1.0 / D, None, ALU.mult),
                     reads=("ps1",), writes=("stats",))
            bld.emit("vector", lambda e: e.tensor_tensor(s2, m_, m_, ALU.mult),
                     reads=("stats",), writes=("stats",))
            bld.emit("vector", lambda e: e.tensor_tensor(s2, q_, s2, ALU.subtract),
                     reads=("stats",), writes=("stats",))
            bld.emit("vector", lambda e: e.tensor_scalar(s2, s2, 1e-5, None, ALU.add),
                     reads=("stats",), writes=("stats",))
            bld.emit("vector", lambda e: e.tensor_scalar(s3i, s2i, 1, None, ALU.logical_shift_right),
                     reads=("stats",), writes=("stats",))
            bld.emit("vector", lambda e: e.tensor_scalar(s3i, s3i, -1, None, ALU.bitwise_xor),
                     reads=("stats",), writes=("stats",))
            bld.emit("vector", lambda e: e.tensor_scalar(s3i, s3i, 0x5f3759df + 1, None, ALU.add),
                     reads=("stats",), writes=("stats",))
            for _nr in range(2):
                bld.emit("vector", lambda e: e.tensor_tensor(q_, s3, s3, ALU.mult),
                         reads=("stats",), writes=("stats",))
                bld.emit("vector", lambda e: e.tensor_tensor(q_, q_, s2, ALU.mult),
                         reads=("stats",), writes=("stats",))
                bld.emit("vector", lambda e: e.tensor_scalar(q_, q_, -0.5, 1.5, ALU.mult, ALU.add),
                         reads=("stats",), writes=("stats",))
                bld.emit("vector", lambda e: e.tensor_tensor(s3, s3, q_, ALU.mult),
                         reads=("stats",), writes=("stats",))
            bld.emit("vector", lambda e: e.tensor_tensor(q_, s3, s3, ALU.mult),
                     reads=("stats",), writes=("stats",))
            bld.emit("vector", lambda e: e.tensor_tensor(q_, q_, s2, ALU.mult),
                     reads=("stats",), writes=("stats",))
            bld.emit("vector", lambda e: e.tensor_scalar(q_, q_, -0.5, 1.5, ALU.mult, ALU.add),
                     reads=("stats",), writes=("stats",))
            bld.emit("vector", lambda e: e.tensor_tensor(a_, s3, q_, ALU.mult),
                     reads=("stats",), writes=("stats",))
            bld.emit("vector", lambda e: e.tensor_tensor(s2, m_, a_, ALU.mult),
                     reads=("stats",), writes=("stats",))
            bld.emit("vector", lambda e: e.tensor_scalar(b_, s2, -1.0, None, ALU.mult),
                     reads=("stats",), writes=("stats",))

        def emit_bcast(i):
            bld.emit("tensor", lambda e: e.matmul(ps[0][:], ones_row[:], a_, start=True, stop=True),
                     reads=("stats", "ones_row"), writes=("ps0",))
            bld.emit("tensor", lambda e: e.matmul(ps[1][:], ones_row[:], b_, start=True, stop=True),
                     reads=("stats", "ones_row"), writes=("ps1",))

        def sh_piece(hsc):
            if hsc < 2:
                return f"sh_p{hsc}"
            if hsc < 4:
                return "sh_p2"
            if hsc < 8:
                return "sh_p3"
            return "sh_p4" if hsc < 12 else "sh_p5"

        def ex_piece(ex):
            if ex < 4:
                return "ex_p0" if ex < 2 else "ex_p1"
            return f"ex_scQ{ex // 4}"

        def dn_piece(oc):
            if oc < 2:
                return f"dn_p{oc}"
            return f"dn_scQ{oc // 2}"

        def emit_sh_dma(hsc, direct=False):
            par = hsc % 3
            bld.emit("sync",
                     lambda e, hsc=hsc, par=par: e.dma_start(
                         out=wsh[par][:], in_=sh_sc[hsc * 128:(hsc + 1) * 128, :]),
                     reads=(sh_piece(hsc),), writes=(f"wsh{par}",), dma=f"d_sh{par}")

        def emit_ex_dma(ex, direct=False):
            par = ex % 2
            bld.emit("sync",
                     lambda e, ex=ex, par=par: e.dma_start(
                         out=wexp[par][:], in_=ex_sc[ex * 128:(ex + 1) * 128, :]),
                     reads=(ex_piece(ex),), writes=(f"wexp{par}",), dma=f"d_ex{par}")

        def emit_dn_dma_c(oc):
            par = oc % 2
            bld.emit("scalar",
                     lambda e, oc=oc, par=par: e.dma_start(
                         out=wdn[par][:, 2048:4096],
                         in_=dn_sc[oc * 128:(oc + 1) * 128, 2048:4096]),
                     reads=(dn_piece(oc),), writes=(f"wdnC{par}",), dma=f"d_dnC{par}")

        def emit_dn_dma(oc, direct=False, with_c=True):
            par = oc % 2
            bld.emit("sync",
                     lambda e, oc=oc, par=par: e.dma_start(
                         out=wdn[par][:, 0:2048],
                         in_=dn_sc[oc * 128:(oc + 1) * 128, 0:2048]),
                     reads=(dn_piece(oc),), writes=(f"wdnA{par}",), dma=f"d_dnA{par}")
            if with_c:
                emit_dn_dma_c(oc)
            bld.emit("sync",
                     lambda e, oc=oc, par=par: e.dma_start(
                         out=wdn[par][:, 4096:8192],
                         in_=dn_sc[oc * 128:(oc + 1) * 128, 4096:8192]),
                     reads=(dn_piece(oc),), writes=(f"wdnB{par}",), dma=f"d_dnB{par}")

        def emit_router(i):
            par = i % 2
            bld.emit("tensor",
                     [(lambda e, kc=kc, par=par: e.matmul(
                         ps[6][0:16, 0:512],
                         rw_bf[:, kc * 16:(kc + 1) * 16],
                         x_sb[par][:, kc * T:(kc + 1) * T],
                         start=(kc == 0), stop=(kc == KC - 1)))
                      for kc in range(KC)],
                     reads=tuple(f"x{par}_{k}" for k in range(KC)) + ("rw_bf",),
                     writes=("ps6",))
            bld.emit("vector", lambda e: e.tensor_copy(xr_r[:], ps[6][0:16, 0:512]),
                     reads=("ps6",), writes=("xr_r",))
            bld.emit("tensor",
                     [(lambda e, ts_=ts_: e.transpose(
                         ps[7][:, 128 + 16 * ts_: 144 + 16 * ts_].bitcast(F32R),
                         xr_r[:, ts_ * 128:(ts_ + 1) * 128], ident_r[:]))
                      for ts_ in range(4)],
                     reads=("xr_r", "ident_r"), writes=("ps7",))
            ab_mms = []
            for ts_ in range(4):
                ab_mms.append(lambda e, ts_=ts_: e.matmul(
                    ps[7][:, 2 * ts_: 2 * ts_ + 1], stats[0:1, 512 + ts_ * 128: 512 + (ts_ + 1) * 128],
                    ones_row[0:1, 0:1], start=(ts_ == 0), stop=False))
                ab_mms.append(lambda e, ts_=ts_: e.matmul(
                    ps[7][:, 2 * ts_ + 1: 2 * ts_ + 2], stats[0:1, 1536 + ts_ * 128: 1536 + (ts_ + 1) * 128],
                    ones_row[0:1, 0:1], start=False, stop=(ts_ == 3)))
            bld.emit("tensor", ab_mms, reads=("stats", "ones_row"), writes=("ps7",))
            bld.emit("vector", lambda e: e.tensor_copy(ab8[:], ps[7][:, 0:8]),
                     reads=("ps7",), writes=("ab8",))
            for ts_ in range(4):
                sc_sl = scores4[:, ts_ * 16:(ts_ + 1) * 16]
                bld.emit("vector",
                         lambda e, ts_=ts_, sc_sl=sc_sl: e.tensor_scalar(
                             sc_sl, ps[7][:, 128 + 16 * ts_: 144 + 16 * ts_], ab8[:, 2 * ts_:2 * ts_ + 1],
                             None, ALU.mult),
                         reads=("ps7", "ab8"), writes=(f"scores_{ts_}",))
                bld.emit("vector",
                         lambda e, ts_=ts_, sc_sl=sc_sl: e.scalar_tensor_tensor(
                             sc_sl, rs16[:], ab8[:, 2 * ts_ + 1:2 * ts_ + 2], sc_sl,
                             ALU.mult, ALU.add),
                         reads=("rs16", "ab8", f"scores_{ts_}"), writes=(f"scores_{ts_}",))



            SCORES = tuple(f"scores_{k}" for k in range(4))
            bld.emit("scalar", lambda e: e.activation(scores4[:], scores4[:], AF.Tanh, scale=0.5),
                     reads=SCORES, writes=SCORES)
            bld.emit("vector",
                     lambda e: e.tensor_scalar(scores4[:], scores4[:], 0.5, 0.5, ALU.mult, ALU.add),
                     reads=SCORES, writes=SCORES)

        def emit_transpose(ts_):
            ct_sl = comb_tm[:, ts_ * 16:(ts_ + 1) * 16]
            bld.emit("tensor",
                     lambda e, ct_sl=ct_sl: e.transpose(ps[0][0:16, 128:192].bitcast(BF16), ct_sl, identbf[:]),
                     reads=("comb_tm_all", "identbf"), writes=("ps0",))
            bld.emit("vector",
                     lambda e, ts_=ts_: e.tensor_copy(comb_fm[:, ts_ * 128:(ts_ + 1) * 128],
                                                      ps[0][0:16, 128:192].bitcast(BF16)),
                     reads=("ps0",), writes=("comb_fm",))

        def emit_chain_all(i):
            SC = scores4[:]
            SCv = SC.rearrange("p (s e) -> p s e", s=4)
            SB = sb64[:]
            SBj = SB.rearrange("p (sg j) -> p sg j", j=4)
            MKv = mask64[:].rearrange("p (s e) -> p s e", s=4)
            CUR = sb64
            CURv = CUR[:].rearrange("p (s e) -> p s e", s=4)
            PENv = pen64[:].rearrange("p (s e) -> p s e", s=4)
            gscv = g16a[:].rearrange("p (s g) -> p s g", g=4)

            def b4(t, n):
                return bass.AP(t, 0, [[4, 128], [1, 4], [0, n]])
            SCORES = tuple(f"scores_{k}" for k in range(4))

            def vop(f, extra_r=(), w=("tkk",)):
                bld.emit("vector", f, reads=("tkk", "bias64") + SCORES + extra_r, writes=w)

            vop(lambda e: e.tensor_tensor(SB, SC, bias64[:], ALU.add))
            vop(lambda e: e.tensor_tensor(g16a[:], SBj[:, :, 0], SBj[:, :, 1], ALU.add))
            vop(lambda e: e.tensor_tensor(g16b[:], SBj[:, :, 2], SBj[:, :, 3], ALU.add))
            vop(lambda e: e.tensor_tensor(g16c[:], SBj[:, :, 0], SBj[:, :, 1], ALU.max))
            vop(lambda e: e.tensor_tensor(g16d[:], SBj[:, :, 2], SBj[:, :, 3], ALU.max))
            vop(lambda e: e.tensor_tensor(g16c[:], g16c[:], g16d[:], ALU.add))
            vop(lambda e: e.tensor_tensor(g16a[:], g16a[:], g16b[:], ALU.max))
            vop(lambda e: e.tensor_tensor(g16a[:], g16a[:], g16c[:], ALU.max))
            vop(lambda e: e.reduce_max(r4a[:], gscv, axis=AX))
            vop(lambda e: e.tensor_tensor(g16b[:].rearrange("p (s g) -> p s g", g=4),
                                          gscv, b4(r4a, 4), ALU.is_ge))
            vop(lambda e: e.tensor_scalar(g16b[:], g16b[:], -BIG, None, ALU.mult))
            vop(lambda e: e.tensor_tensor(g16b[:], g16a[:], g16b[:], ALU.add))
            vop(lambda e: e.reduce_max(r4b[:], g16b[:].rearrange("p (s g) -> p s g", g=4), axis=AX))
            vop(lambda e: e.tensor_tensor(g16c[:].rearrange("p (s g) -> p s g", g=4),
                                          gscv, b4(r4b, 4), ALU.is_ge))
            vop(lambda e: e.tensor_copy(mask64[:].rearrange("p (sg j) -> p sg j", j=4),
                                        bass.AP(g16c, 0, [[16, 128], [1, 16], [0, 4]])))
            vop(lambda e: e.tensor_scalar(mask64[:], mask64[:], BIG, -BIG, ALU.mult, ALU.add))
            vop(lambda e: e.tensor_tensor(mask64[:], SB, mask64[:], ALU.add))
            vop(lambda e: e.tensor_copy(CUR[:], mask64[:]))
            for _r in range(3):
                vop(lambda e: e.reduce_max(r4a[:], CURv, axis=AX))
                vop(lambda e: e.tensor_tensor(PENv, CURv, b4(r4a, 16), ALU.is_ge))
                vop(lambda e: e.tensor_scalar(pen64[:], pen64[:], -BIG, None, ALU.mult))
                vop(lambda e: e.tensor_tensor(CUR[:], CUR[:], pen64[:], ALU.add))
            vop(lambda e: e.reduce_max(r4a[:], CURv, axis=AX))
            vop(lambda e: e.tensor_tensor(PENv, MKv, b4(r4a, 16), ALU.is_ge))
            vop(lambda e: e.tensor_tensor(CUR[:], SC, pen64[:], ALU.mult))
            vop(lambda e: e.tensor_reduce(r4a[:], CURv, axis=AX, op=ALU.add))
            vop(lambda e: e.tensor_scalar(r4a[:], r4a[:], 1e-20, None, ALU.add))
            vop(lambda e: e.reciprocal(r4b[:], r4a[:]))
            bld.emit("vector",
                     lambda e: e.tensor_tensor(comb_tm[:].rearrange("p (s e) -> p s e", s=4),
                                               CURv, b4(r4b, 16), ALU.mult),
                     reads=("tkk",) + SCORES, writes=("comb_tm_all",))

        def emit_ln(i):
            par = i % 2
            for kc in range(KC):
                tb = silu_sb[kc % 2]
                bld.emit("vector",
                         lambda e, kc=kc, tb=tb, par=par: e.tensor_tensor(tb[:], xch(x_sb[par], kc), ps[0][:], ALU.mult),
                         reads=(f"x{par}_{kc}", "ps0"), writes=(f"silu{kc % 2}",))
                bld.emit("vector",
                         lambda e, i=i, kc=kc, tb=tb: e.tensor_tensor(xnbf_tok(i, kc), tb[:], ps[1][:], ALU.add),
                         reads=(f"silu{kc % 2}", "ps1"), writes=(f"xnbf{i % 2}_{kc}",))

        def emit_conv_taps(i, kc):
            hv = hcv2[kc % 2]
            bld.emit("vector",
                     lambda e, kc=kc, i=i, hv=hv: e.tensor_scalar(
                         hv[:], xnbf_shift(i, kc, 17),
                         et_sb[:, kc * 21 + 17: kc * 21 + 18], None, ALU.mult),
                     reads=(f"xnbf{i % 2}_{kc}", "et_sb"), writes=(f"hcv{kc % 2}",))
            for j in range(18, 21):
                bld.emit("vector",
                         lambda e, kc=kc, j=j, i=i, hv=hv: e.scalar_tensor_tensor(
                             hv[:], xnbf_shift(i, kc, j),
                             et_sb[:, kc * 21 + j: kc * 21 + j + 1], hv[:],
                             ALU.mult, ALU.add),
                         reads=(f"xnbf{i % 2}_{kc}", "et_sb", f"hcv{kc % 2}"), writes=(f"hcv{kc % 2}",))

        def emit_halo(i):
            p = i % 2
            q = (i - 1) % 2
            bld.emit("scalar",
                     [(lambda e, kc=kc, p=p, q=q: e.copy(
                         xn_bf2[p][:, kc * CW + HALO - TAPS: kc * CW + HALO],
                         xn_bf2[q][:, kc * CW + HALO + T - TAPS: kc * CW + HALO + T]))
                      for kc in range(KC)],
                     reads=XNBF(i - 1), writes=XNBF(i))

        emit_x_load(0)
        cast_pieces = [
            ("sh_p0", sh_sc[0:128, :], shp[0:128, :]),
            ("sh_p1", sh_sc[128:256, :], shp[128:256, :]),
            ("sh_p2", sh_sc[256:512, :], shp[256:512, :]),
            ("sh_p3", sh_sc[512:1024, :], shp[512:1024, :]),
            ("sh_p4", sh_sc[1024:1536, :], shp[1024:1536, :]),
            ("sh_p5", sh_sc[1536:2048, :], shp[1536:2048, :]),
            ("ex_p0", ex_sc[0:256, :], exp_p[0:256, :]),
            ("ex_p1", ex_sc[256:512, :], exp_p[256:512, :]),
            ("ex_scQ1", ex_sc[512:1024, :], exp_p[512:1024, :]),
            ("dn_p0", dn_sc[0:128, :], dnp[0:128, :]),
            ("dn_p1", dn_sc[128:256, :], dnp[128:256, :]),
            ("ex_scQ2", ex_sc[1024:1536, :], exp_p[1024:1536, :]),
            ("dn_scQ1", dn_sc[256:512, :], dnp[256:512, :]),
            ("ex_scQ3", ex_sc[1536:2048, :], exp_p[1536:2048, :]),
            ("dn_scQ2", dn_sc[512:768, :], dnp[512:768, :]),
            ("dn_scQ3", dn_sc[768:1024, :], dnp[768:1024, :]),
        ]
        for ci, (res, dst, src) in enumerate(cast_pieces):
            bld.emit("gpsimd", lambda e, dst=dst, src=src: e.dma_start(out=dst, in_=src),
                     writes=(res,), dma=f"d_c{ci}")
            if res == "ex_p1":
                emit_x_load(1)
        emit_sum_mm(0)
        for kc in range(KC):
            emit_square(0, kc)
            emit_sq_mm(kc)
        emit_stats_math(0)
        emit_bcast(0)
        emit_router(0)
        emit_chain_all(0)
        emit_ln(0)
        emit_conv_taps(0, 0)
        emit_conv_taps(0, 1)

        def emit_tile_prologue(it2):
            emit_diag_build(0)
            emit_diag_build(1)
            emit_diag_build(2)
            emit_diag_build(3)
            emit_sh_dma(0)
            emit_sh_dma(1)
            emit_sh_dma(2)
            emit_ex_dma(0)
            emit_ex_dma(1)

        for it in range(n_tiles):
            t0 = it * T
            par = it % 2
            if it == 0:
                emit_tile_prologue(0)

            for kc in range(KC):
                par_d = kc % 4
                pb = 2 + kc % 4
                bld.emit("tensor",
                         [(lambda e, j=j, kc=kc, par_d=par_d, pb=pb, it=it: e.matmul(
                             ps[pb][:], diag[par_d][:, (j - 1) * 128: j * 128], xnbf_shift(it, kc, j),
                             start=(j == 1), stop=(j == 16)))
                          for j in range(1, 17)],
                         reads=(f"diag{par_d}", f"xnbf{it % 2}_{kc}"), writes=(f"ps{pb}",))
                if kc >= 2:
                    emit_conv_taps(it, kc)
                bld.emit("vector",
                         lambda e, kc=kc, pb=pb, hv=hcv2[kc % 2]: e.tensor_tensor(
                             xch(h_conv, kc), ps[pb][:], hv[:], ALU.add),
                         reads=(f"ps{pb}", f"hcv{kc % 2}"), writes=("h_conv",))
                if kc + 4 < KC:
                    emit_diag_build(kc + 4)
            if 1 <= it < n_tiles - 1:
                emit_x_load(it + 1)

            for hsc in range(16):
                par_s = hsc % 3
                sl = hsc % 2
                pg = (6, 2, 3)[hsc % 3]
                pu = (7, 4, 5)[hsc % 3]
                bld.emit("tensor",
                         [(lambda e, kc=kc, pg=pg, par_s=par_s, it=it: e.matmul(
                             ps[pg][:], wsh[par_s][:, kc * 128:(kc + 1) * 128], xnbf_tok(it, kc),
                             start=(kc == 0), stop=(kc == KC - 1)))
                          for kc in range(KC)],
                         reads=(f"wsh{par_s}",) + XNBF(it), writes=(f"ps{pg}",))
                bld.emit("tensor",
                         [(lambda e, kc=kc, pu=pu, par_s=par_s, it=it: e.matmul(
                             ps[pu][:], wsh[par_s][:, 1024 + kc * 128: 1024 + (kc + 1) * 128], xnbf_tok(it, kc),
                             start=(kc == 0), stop=(kc == KC - 1)))
                          for kc in range(KC)],
                         reads=(f"wsh{par_s}",) + XNBF(it), writes=(f"ps{pu}",))
                bld.emit("scalar", lambda e, pg=pg, sl=sl: e.activation(silu_sb[sl][:], ps[pg][:], AF.Silu),
                         reads=(f"ps{pg}",), writes=(f"silu{sl}",))
                bld.emit("vector", lambda e, pu=pu, sl=sl: e.tensor_copy(u_sb[sl][:], ps[pu][:]),
                         reads=(f"ps{pu}",), writes=(f"usb{sl}",))
                bld.emit("gpsimd",
                         lambda e, hsc=hsc, sl=sl: e.tensor_tensor(
                             h_s[:, hsc * T:(hsc + 1) * T], silu_sb[sl][:], u_sb[sl][:], ALU.mult),
                         reads=(f"silu{sl}", f"usb{sl}"), writes=("h_s",))
                if hsc in (5, 7, 9, 11):
                    emit_transpose((hsc - 5) // 2)
                if hsc + 3 < 16:
                    emit_sh_dma(hsc + 3, direct=(it == 0 and hsc + 3 >= 8))

            for ex in range(NE):
                par_e = ex % 2
                pc = 6 + ex % 2
                bld.emit("tensor",
                         lambda e, ex=ex, pc=pc: e.matmul(ps[pc][:], sel_sb[:, ex * 128:(ex + 1) * 128],
                                                          comb_fm[:, :], start=True, stop=True),
                         reads=("sel_sb", "comb_fm"), writes=(f"ps{pc}",))
                for hc in range(2):
                    pg = 2 + hc
                    pu = 4 + hc
                    bld.emit("tensor",
                             [(lambda e, kc=kc, hc=hc, pg=pg, par_e=par_e, it=it: e.matmul(
                                 ps[pg][:], wexp[par_e][:, kc * 256 + hc * 128: kc * 256 + hc * 128 + 128],
                                 xnbf_tok(it, kc), start=(kc == 0), stop=(kc == KC - 1)))
                              for kc in range(KC)],
                             reads=(f"wexp{par_e}",) + XNBF(it), writes=(f"ps{pg}",))
                    bld.emit("tensor",
                             [(lambda e, kc=kc, hc=hc, pu=pu, par_e=par_e, it=it: e.matmul(
                                 ps[pu][:], wexp[par_e][:, 2048 + kc * 256 + hc * 128: 2048 + kc * 256 + hc * 128 + 128],
                                 xnbf_tok(it, kc), start=(kc == 0), stop=(kc == KC - 1)))
                              for kc in range(KC)],
                             reads=(f"wexp{par_e}",) + XNBF(it), writes=(f"ps{pu}",))
                    bld.emit("scalar", lambda e, pg=pg, hc=hc: e.activation(silu_sb[hc][:], ps[pg][:], AF.Silu),
                             reads=(f"ps{pg}",), writes=(f"silu{hc}",))
                    bld.emit("vector", lambda e, pu=pu, hc=hc: e.tensor_copy(u_sb[hc][:], ps[pu][:]),
                             reads=(f"ps{pu}",), writes=(f"usb{hc}",))
                    bld.emit("gpsimd",
                             lambda e, hc=hc: e.tensor_tensor(u_sb[hc][:], silu_sb[hc][:], u_sb[hc][:], ALU.mult),
                             reads=(f"silu{hc}", f"usb{hc}"), writes=(f"usb{hc}",))
                    bld.emit("vector",
                             lambda e, ex=ex, hc=hc, pc=pc: e.tensor_tensor(
                                 h_w[:, (ex * 2 + hc) * T:(ex * 2 + hc + 1) * T],
                                 u_sb[hc][:], ps[pc][:], ALU.mult),
                             reads=(f"usb{hc}", f"ps{pc}"), writes=(f"hw{ex}",))
                if ex + 2 < NE:
                    emit_ex_dma(ex + 2, direct=(it == 0 and ex + 2 >= 8))

            emit_dn_dma(0)
            emit_dn_dma(1)
            if it + 1 < n_tiles:
                emit_tile_prologue(it + 1)

            for oc in range(8):
                po = 2 + oc % 2
                par_o = oc % 2
                w0 = wdn[par_o]
                mms = []
                for kc in range(KC):
                    mms.append(lambda e, kc=kc, po=po, w0=w0, it=it: e.matmul(
                        ps[po][:], w0[:, kc * 128:(kc + 1) * 128],
                        xnbf_tok(it, kc), start=(kc == 0), stop=False))
                for kc in range(KC):
                    mms.append(lambda e, kc=kc, po=po, w0=w0, it=it: e.matmul(
                        ps[po][:], w0[:, 1024 + kc * 128: 1024 + (kc + 1) * 128],
                        xch(h_conv, kc), start=False, stop=False))
                for hsc in range(16):
                    mms.append(lambda e, hsc=hsc, po=po, w0=w0: e.matmul(
                        ps[po][:], w0[:, 2048 + hsc * 128: 2048 + (hsc + 1) * 128],
                        h_s[:, hsc * T:(hsc + 1) * T], start=False, stop=False))
                bld.emit("tensor", mms[:16],
                         reads=(f"wdnA{par_o}", "h_conv") + XNBF(it),
                         writes=(f"ps{po}",))
                bld.emit("tensor", mms[16:],
                         reads=(f"wdnA{par_o}", f"wdnC{par_o}", "h_s"),
                         writes=(f"ps{po}",))
                if oc + 2 < 8:
                    emit_dn_dma_c(oc + 2)
                mmsb = []
                for eh in range(32):
                    mmsb.append(lambda e, eh=eh, po=po, w0=w0: e.matmul(
                        ps[po][:], w0[:, 4096 + eh * 128: 4096 + (eh + 1) * 128],
                        h_w[:, eh * T:(eh + 1) * T], start=False, stop=(eh == 31)))
                bld.emit("tensor", mmsb[:16],
                         reads=(f"wdnB{par_o}",) + tuple(f"hw{ex2}" for ex2 in range(8)),
                         writes=(f"ps{po}",))
                bld.emit("tensor", mmsb[16:],
                         reads=tuple(f"hw{ex2}" for ex2 in range(8, 16)),
                         writes=(f"ps{po}",))
                ostage = out_st[oc % 2]
                oname = f"out_st{oc % 2}"
                etmp = silu_sb[oc % 2]
                ename = f"silu{oc % 2}"
                bld.emit("scalar",
                         lambda e, oc=oc, po=po, etmp=etmp: e.activation(
                             etmp[:], ps[po][:], AF.Identity, bias=bias_col[:, oc:oc + 1]),
                         reads=(f"ps{po}", "bias_col"), writes=(ename,))
                bld.emit("gpsimd",
                         lambda e, oc=oc, ostage=ostage, etmp=etmp, par=par: e.tensor_tensor(
                             ostage[:], etmp[:], xch(x_sb[par], oc), ALU.add),
                         reads=(ename, f"x{par}_{oc}"), writes=(oname,))
                odst = out_fm[oc * 128:(oc + 1) * 128, t0:t0 + T]
                bld.emit("sync",
                         lambda e, odst=odst, ostage=ostage: e.dma_start(out=odst, in_=ostage[:]),
                         reads=(oname,), dma=("d_out0" if oc % 2 == 0 else "d_out1"))
                if oc + 2 < 8:
                    emit_dn_dma(oc + 2, with_c=False)
                if it + 1 < n_tiles:
                    if oc == 0:
                        emit_sum_mm(it + 1)
                        for kc2 in range(KC):
                            emit_square(it + 1, kc2)
                            emit_sq_mm(kc2)
                    elif oc == 1:
                        emit_stats_math(it + 1)
                    elif oc == 2:
                        emit_bcast(it + 1)
                    elif oc == 4:
                        emit_ln(it + 1)
                    elif oc == 5:
                        emit_router(it + 1)
                        emit_halo(it + 1)
                    elif oc == 6:
                        emit_chain_all(it + 1)
                    elif oc == 7:
                        emit_conv_taps(it + 1, 0)
                        emit_conv_taps(it + 1, 1)
        bld.finalize_waits("sync")

        @block.sync
        def _(eng):
            bld.replay("sync", eng)

        @block.tensor
        def _(eng):
            bld.replay("tensor", eng)

        @block.vector
        def _(eng):
            bld.replay("vector", eng)

        @block.scalar
        def _(eng):
            bld.replay("scalar", eng)

        @block.gpsimd
        def _(eng):
            bld.replay("gpsimd", eng)

    return nc


def host_prep(inputs):
    f32 = np.float32
    x = np.asarray(inputs["L0_attn_input_data"], dtype=f32)
    W1 = np.asarray(inputs["L0_attn_W1"], dtype=f32)
    W2 = np.asarray(inputs["L0_attn_W2"], dtype=f32)
    bias = np.asarray(inputs["L0_attn_bias"], dtype=f32)
    et = np.asarray(inputs["L0_attn_embeddings_table"], dtype=f32)
    rw = np.asarray(inputs["L0_ffn_router_w"], dtype=f32)
    cb = np.asarray(inputs["L0_ffn_correction_bias"], dtype=f32)
    gw = np.asarray(inputs["L0_ffn_gate_w"], dtype=f32)
    uw = np.asarray(inputs["L0_ffn_up_w"], dtype=f32)
    dw = np.asarray(inputs["L0_ffn_down_w"], dtype=f32)
    sgw = np.asarray(inputs["L0_ffn_shared_gate_w"], dtype=f32)
    suw = np.asarray(inputs["L0_ffn_shared_up_w"], dtype=f32)
    sdw = np.asarray(inputs["L0_ffn_shared_down_w"], dtype=f32)

    c = np.ascontiguousarray
    sg_l = sgw.reshape(16, 128, 8, 128).transpose(0, 3, 2, 1).reshape(16, 128, 1024)
    su_l = suw.reshape(16, 128, 8, 128).transpose(0, 3, 2, 1).reshape(16, 128, 1024)
    shp = c(np.concatenate([sg_l, su_l], axis=2).reshape(2048, 2048))
    gw_l = gw.reshape(16, 8, 128, 2, 128).transpose(0, 2, 1, 3, 4).reshape(16, 128, 2048)
    uw_l = uw.reshape(16, 8, 128, 2, 128).transpose(0, 2, 1, 3, 4).reshape(16, 128, 2048)
    exp_p = c(np.concatenate([gw_l, uw_l], axis=2).reshape(2048, 4096))
    w1_l = W1.reshape(8, 128, 8, 128).transpose(2, 1, 0, 3).reshape(8, 128, 1024)
    w2_l = W2.reshape(8, 128, 8, 128).transpose(2, 1, 0, 3).reshape(8, 128, 1024)
    sdw_l = sdw.reshape(8, 128, 16, 128).transpose(0, 3, 2, 1).reshape(8, 128, 2048)
    dw_l = dw.reshape(16, 2, 128, 8, 128).transpose(3, 2, 0, 1, 4).reshape(8, 128, 4096)
    dnp = c(np.concatenate([w1_l, w2_l, sdw_l, dw_l], axis=2).reshape(1024, 8192))

    shared = {
        "shp": shp,
        "exp_p": exp_p,
        "dnp": dnp,
        "etp": c(et.T.reshape(8, 128, 21).transpose(1, 0, 2).reshape(128, 168)),
        "rwp": c(rw.T.reshape(8, 128, 16).transpose(1, 0, 2).reshape(128, 128)),
        "biasp": c(bias.reshape(8, 128).T),
        "cbp": c(cb.reshape(1, 16)),
        "identbf": np.eye(128, dtype=ml_dtypes.bfloat16),
    }
    sel = np.zeros((16, 2048), dtype=ml_dtypes.bfloat16)
    for e in range(16):
        sel[e, e * 128:(e + 1) * 128] = 1.0
    shared["selp"] = sel
    per_core = [dict(shared, x_fm=c(x[cidx].T)) for cidx in range(B)]
    return per_core


def kernel(**inputs):
    in_maps = host_prep(inputs)
    nc = build_program(NT)
    res = run_bass_kernel_spmd(nc, in_maps, core_ids=list(range(B)))
    out = np.stack([np.asarray(r["out_fm"], dtype=np.float32).T for r in res.results])
    return np.ascontiguousarray(out)


# version 44
# speedup vs baseline: 1.0761x; 1.0761x over previous
import numpy as np
import ml_dtypes
from contextlib import ExitStack

import concourse.bass as bass
import concourse.mybir as mybir
from concourse.bass_utils import run_bass_kernel_spmd

F32 = mybir.dt.float32
F32R = mybir.dt.float32r
BF16 = mybir.dt.bfloat16
AF = mybir.ActivationFunctionType
ALU = mybir.AluOpType
AX = mybir.AxisListType.X

B, N, D = 8, 4096, 1024
T = 512
KC = D // 128
NT = N // T
HS = 2048
NE = 16
H = 256
TAPS = 20
HALO = 32
CW = T + HALO
BIG = 1.0e30

ENGINES = ("sync", "tensor", "vector", "scalar", "gpsimd")
ENG_SEM = {"tensor": "pe", "vector": "ve", "scalar": "se", "gpsimd": "po"}
DMA_SEMS = ("d_x0", "d_x1", "d_sh0", "d_sh1", "d_sh2", "d_ex0", "d_ex1", "d_dnA0", "d_dnA1", "d_dnC0", "d_dnC1", "d_dnB0", "d_dnB1",
             "d_out0", "d_out1", "d_const") + tuple(f"d_c{i}" for i in range(16))


class Build:

    def __init__(self):
        self.ops = {e: [] for e in ENGINES}
        self.sems = {}
        self.counts = {}
        for name in list(ENG_SEM.values()) + list(DMA_SEMS):
            self.counts[name] = 0
        self.last_wait = {e: {} for e in ENGINES}
        self.res = {}

    def _wait(self, eng, sem_name, val):
        if val <= 0:
            return
        lw = self.last_wait[eng].get(sem_name, 0)
        if val <= lw:
            return
        self.last_wait[eng][sem_name] = val
        self.ops[eng].append(("wait", sem_name, val))

    def emit(self, eng, fns, reads=(), writes=(), dma=None):
        if not isinstance(fns, (list, tuple)):
            fns = [fns]
        if dma is not None:
            self._wait(eng, dma, self.counts[dma])
        for r in reads:
            rec = self.res.get(r)
            if rec and rec.get("w"):
                s, v = rec["w"]
                self._wait(eng, s, v)
        for r in writes:
            rec = self.res.get(r)
            if rec:
                if rec.get("w"):
                    s, v = rec["w"]
                    self._wait(eng, s, v)
                for (s, v) in rec.get("r", []):
                    self._wait(eng, s, v)
        if dma is not None:
            self.counts[dma] += 16 * len(fns)
            val = self.counts[dma]
            self.ops[eng].append(("dma", fns, dma))
            ev = (dma, val)
        else:
            sem_name = ENG_SEM[eng]
            self.counts[sem_name] += 1
            val = self.counts[sem_name]
            self.ops[eng].append(("ops", fns, sem_name))
            ev = (sem_name, val)
        for r in reads:
            rec = self.res.setdefault(r, {"w": None, "r": []})
            rec["r"].append(ev)
        for r in writes:
            rec = self.res.setdefault(r, {"w": None, "r": []})
            rec["w"] = ev
            rec["r"] = []
        return val

    def finalize_waits(self, eng):
        for s, v in sorted(self.counts.items()):
            self._wait(eng, s, v)

    def replay(self, eng_name, eng):
        for item in self.ops[eng_name]:
            if item[0] == "wait":
                _, s, v = item
                eng.wait_ge(self.sems[s], v)
            elif item[0] == "dma":
                _, fns, sem_name = item
                for f in fns:
                    f(eng).then_inc(self.sems[sem_name], 16)
            else:
                _, fns, sem_name = item
                ins = None
                for f in fns:
                    ins = f(eng)
                assert ins is not None
                ins.then_inc(self.sems[sem_name], 1)


def build_program(n_tiles=NT):
    nc = bass.Bass("TRN2", target_bir_lowering=False)

    x_fm = nc.dram_tensor("x_fm", [D, N], F32, kind="ExternalInput").ap()
    shp = nc.dram_tensor("shp", [16 * 128, 2048], F32, kind="ExternalInput").ap()
    exp_p = nc.dram_tensor("exp_p", [16 * 128, 4096], F32, kind="ExternalInput").ap()
    dnp = nc.dram_tensor("dnp", [8 * 128, 8192], F32, kind="ExternalInput").ap()
    etp = nc.dram_tensor("etp", [128, 8 * 21], F32, kind="ExternalInput").ap()
    rwp = nc.dram_tensor("rwp", [128, 8 * 16], F32, kind="ExternalInput").ap()
    biasp = nc.dram_tensor("biasp", [128, 8], F32, kind="ExternalInput").ap()
    cbp = nc.dram_tensor("cbp", [1, 16], F32, kind="ExternalInput").ap()
    selp = nc.dram_tensor("selp", [16, 16 * 128], BF16, kind="ExternalInput").ap()
    identbf_in = nc.dram_tensor("identbf", [128, 128], BF16, kind="ExternalInput").ap()

    out_fm = nc.dram_tensor("out_fm", [D, N], F32, kind="ExternalOutput").ap()

    sh_sc = nc.dram_tensor("sh_sc", [16 * 128, 2048], BF16, kind="Internal").ap()
    ex_sc = nc.dram_tensor("ex_sc", [16 * 128, 4096], BF16, kind="Internal").ap()
    dn_sc = nc.dram_tensor("dn_sc", [8 * 128, 8192], BF16, kind="Internal").ap()

    bld = Build()

    with ExitStack() as ctx:
        cc = ctx.enter_context
        x_sb = [cc(nc.sbuf_tensor(f"x_sb{p}", [128, KC * T], BF16)) for p in range(2)]
        xn_bf2 = [cc(nc.sbuf_tensor(f"xn_bf{p}", [128, KC * CW], BF16)) for p in range(2)]
        h_conv = cc(nc.sbuf_tensor("h_conv", [128, KC * T], BF16))
        h_s = cc(nc.sbuf_tensor("h_s", [128, 16 * T], BF16))
        h_w = cc(nc.sbuf_tensor("h_w", [128, 32 * T], BF16))
        silu_sb = [cc(nc.sbuf_tensor(f"silu{p}", [128, T], F32)) for p in range(2)]
        u_sb = [cc(nc.sbuf_tensor(f"usb{p}", [128, T], F32)) for p in range(2)]
        xr2 = [cc(nc.sbuf_tensor(f"xr{p}", [128, T], BF16)) for p in range(2)]
        out_st = [cc(nc.sbuf_tensor(f"out_st{p}", [128, T], F32)) for p in range(2)]
        wsh = [cc(nc.sbuf_tensor(f"wsh{p}", [128, 2048], BF16)) for p in range(3)]
        wexp = [cc(nc.sbuf_tensor(f"wexp{p}", [128, 4096], BF16)) for p in range(2)]
        wdn = [cc(nc.sbuf_tensor(f"wdn{p}", [128, 8192], BF16)) for p in range(2)]
        diag = [cc(nc.sbuf_tensor(f"diag{p}", [128, TAPS * 128], BF16)) for p in range(4)]
        et_sb = cc(nc.sbuf_tensor("et_sb", [128, 8 * 21], F32))
        rw_sb = cc(nc.sbuf_tensor("rw_sb", [128, 128], F32))
        xr_r = cc(nc.sbuf_tensor("xr_r", [16, 512], F32R))
        rw_bf = cc(nc.sbuf_tensor("rw_bf", [128, 128], BF16))
        hcv2 = [cc(nc.sbuf_tensor(f"hcv{p}", [128, T], F32)) for p in range(2)]
        ident_r = cc(nc.sbuf_tensor("ident_r", [16, 16], F32R))
        bias_col = cc(nc.sbuf_tensor("bias_col", [128, 8], F32))
        cb_sb = cc(nc.sbuf_tensor("cb_sb", [1, 16], F32))
        bias16 = cc(nc.sbuf_tensor("bias16", [128, 16], F32))
        rs16 = cc(nc.sbuf_tensor("rs16", [128, 16], F32))
        sel_sb = cc(nc.sbuf_tensor("sel_sb", [16, 2048], BF16))
        identbf = cc(nc.sbuf_tensor("identbf_sb", [128, 128], BF16))
        ones_col = cc(nc.sbuf_tensor("ones_col", [128, 1], F32))
        ones_col_r = cc(nc.sbuf_tensor("ones_col_r", [128, 1], BF16))
        ones_row = cc(nc.sbuf_tensor("ones_row", [1, 128], F32))
        ab8 = cc(nc.sbuf_tensor("ab8", [128, 8], F32))
        stats = cc(nc.sbuf_tensor("stats", [1, 2048], F32))
        scores4 = cc(nc.sbuf_tensor("scores4", [128, 64], F32))
        sb64 = cc(nc.sbuf_tensor("sb64", [128, 64], F32))
        mask64 = cc(nc.sbuf_tensor("mask64", [128, 64], F32))
        pen64 = cc(nc.sbuf_tensor("pen64", [128, 64], F32))
        bias64 = cc(nc.sbuf_tensor("bias64", [128, 64], F32))
        g16a = cc(nc.sbuf_tensor("g16a", [128, 16], F32))
        g16b = cc(nc.sbuf_tensor("g16b", [128, 16], F32))
        g16c = cc(nc.sbuf_tensor("g16c", [128, 16], F32))
        g16d = cc(nc.sbuf_tensor("g16d", [128, 16], F32))
        r4a = cc(nc.sbuf_tensor("r4a", [128, 4], F32))
        r4b = cc(nc.sbuf_tensor("r4b", [128, 4], F32))
        comb_fm = cc(nc.sbuf_tensor("comb_fm", [16, T], BF16))
        comb_tm = cc(nc.sbuf_tensor("comb_tm", [128, 64], BF16))

        ps = [cc(nc.psum_tensor(f"ps{i}", [128, 512], F32)) for i in range(8)]

        for name in list(ENG_SEM.values()) + list(DMA_SEMS):
            bld.sems[name] = cc(nc.semaphore(name))

        block = cc(nc.Block())

        def xch(buf, kc):
            return buf[:, kc * T:(kc + 1) * T]

        def xnbf_tok(i, kc):
            return xn_bf2[i % 2][:, kc * CW + HALO: kc * CW + HALO + T]

        def xnbf_shift(i, kc, j):
            return xn_bf2[i % 2][:, kc * CW + HALO - j: kc * CW + HALO - j + T]

        def XNBF(i):
            return tuple(f"xnbf{i % 2}_{k}" for k in range(KC))

        m_ = stats[:, 0:512]
        q_ = stats[:, 512:1024]
        s2 = stats[:, 1024:1536]
        s3 = stats[:, 1536:2048]
        a_ = stats[:, 512:1024]
        b_ = stats[:, 1536:2048]
        s2i = stats[:, 1024:1536].bitcast(mybir.dt.int32)
        s3i = stats[:, 1536:2048].bitcast(mybir.dt.int32)

        const_pairs = [
            (et_sb[:], etp[:, :], "et_sb"),
            (rw_sb[:], rwp[:, :], "rw_sb"),
            (bias_col[:], biasp[:, :], "bias_col"),
            (cb_sb[:], cbp[:, :], "cb_sb"),
            (sel_sb[:], selp[:, :], "sel_sb"),
            (identbf[:], identbf_in[:, :], "identbf"),
        ]
        bld.emit("sync",
                 [(lambda e, d=d, s=s: e.dma_start(out=d, in_=s)) for d, s, _ in const_pairs],
                 writes=tuple(r for _, _, r in const_pairs), dma="d_const")
        bld.emit("vector", lambda e: e.memset(ones_col[:], 1.0), writes=("ones_col",))
        bld.emit("scalar", lambda e: e.copy(ones_col_r[:], ones_col[:]),
                 reads=("ones_col",), writes=("ones_col_r",))
        bld.emit("scalar", lambda e: e.copy(ident_r[:], identbf[0:16, 0:16]),
                 reads=("identbf",), writes=("ident_r",))
        bld.emit("scalar", lambda e: e.copy(rw_bf[:], rw_sb[:]),
                 reads=("rw_sb",), writes=("rw_bf",))
        bld.emit("vector", lambda e: e.memset(ones_row[:], 1.0), writes=("ones_row",))
        bld.emit("vector", lambda e: e.memset(xn_bf2[0][:], 0.0), writes=XNBF(0))
        bld.emit("vector", lambda e: e.memset(xn_bf2[1][:], 0.0), writes=XNBF(1))
        bld.emit("tensor",
                 lambda e: e.matmul(ps[6][:, 0:16], ones_row[:], cb_sb[:], start=True, stop=True),
                 reads=("ones_row", "cb_sb"), writes=("ps6",))
        bld.emit("vector", lambda e: e.tensor_copy(bias16[:], ps[6][:, 0:16]),
                 reads=("ps6",), writes=("bias16",))
        for _s in range(4):
            bld.emit("vector",
                     lambda e, _s=_s: e.tensor_copy(bias64[:, _s * 16:(_s + 1) * 16], bias16[:]),
                     reads=("bias16",), writes=("bias64",))
        bld.emit("tensor",
                 [(lambda e, kc=kc: e.matmul(ps[7][0:1, 0:16], ones_col_r[:],
                                             rw_bf[:, kc * 16:(kc + 1) * 16],
                                             start=(kc == 0), stop=(kc == KC - 1)))
                  for kc in range(KC)],
                 reads=("rw_bf", "ones_col_r"), writes=("ps7",))
        bld.emit("vector", lambda e: e.tensor_copy(stats[:, 0:16], ps[7][0:1, 0:16]),
                 reads=("ps7",), writes=("stats",))
        bld.emit("tensor",
                 lambda e: e.matmul(ps[6][:, 0:16], ones_row[:], stats[0:1, 0:16], start=True, stop=True),
                 reads=("ones_row", "stats"), writes=("ps6",))
        bld.emit("vector", lambda e: e.tensor_copy(rs16[:], ps[6][:, 0:16]),
                 reads=("ps6",), writes=("rs16",))
        def emit_diag_build(kc):
            par = kc % 4
            bld.emit("vector",
                     [(lambda e, j=j, kc=kc, par=par: e.tensor_scalar(
                         diag[par][:, (j - 1) * 128: j * 128], identbf[:],
                         et_sb[:, kc * 21 + j: kc * 21 + j + 1], None, ALU.mult))
                      for j in range(1, 17)],
                     reads=("identbf", "et_sb"), writes=(f"diag{par}",))

        def emit_x_load(i):
            par = i % 2
            t0 = i * T
            bld.emit("gpsimd",
                     [(lambda e, kc=kc, par=par, s=x_fm[kc * 128:(kc + 1) * 128, t0:t0 + T]:
                       e.dma_start(out=xch(x_sb[par], kc), in_=s)) for kc in range(KC)],
                     writes=tuple(f"x{par}_{k}" for k in range(KC)), dma=f"d_x{par}")

        def emit_square(i, kc):
            par = i % 2
            tb = xr2[kc % 2]
            bld.emit("scalar", lambda e, kc=kc, tb=tb, par=par: e.square(tb[:], xch(x_sb[par], kc)),
                     reads=(f"x{par}_{kc}",), writes=(f"xr{kc % 2}",))



        def emit_sum_mm(i):
            par = i % 2
            bld.emit("tensor",
                     [(lambda e, kc=kc, par=par: e.matmul(ps[0][0:1, :], ones_col_r[:], xch(x_sb[par], kc),
                                                          start=(kc == 0), stop=(kc == KC - 1)))
                      for kc in range(KC)],
                     reads=tuple(f"x{par}_{k}" for k in range(KC)) + ("ones_col_r",),
                     writes=("ps0",))

        def emit_sq_mm(kc):
            tb = xr2[kc % 2]
            bld.emit("tensor",
                     lambda e, kc=kc, tb=tb: e.matmul(ps[1][0:1, :], ones_col_r[:], tb[:],
                                                      start=(kc == 0), stop=(kc == KC - 1)),
                     reads=(f"xr{kc % 2}", "ones_col_r"), writes=("ps1",))

        def emit_stats_math(i):
            bld.emit("vector", lambda e: e.tensor_scalar(m_, ps[0][0:1, :], 1.0 / D, None, ALU.mult),
                     reads=("ps0",), writes=("stats",))
            bld.emit("vector", lambda e: e.tensor_scalar(q_, ps[1][0:1, :], 1.0 / D, None, ALU.mult),
                     reads=("ps1",), writes=("stats",))
            bld.emit("vector", lambda e: e.tensor_tensor(s2, m_, m_, ALU.mult),
                     reads=("stats",), writes=("stats",))
            bld.emit("vector", lambda e: e.tensor_tensor(s2, q_, s2, ALU.subtract),
                     reads=("stats",), writes=("stats",))
            bld.emit("vector", lambda e: e.tensor_scalar(s2, s2, 1e-5, None, ALU.add),
                     reads=("stats",), writes=("stats",))
            bld.emit("vector", lambda e: e.tensor_scalar(s3i, s2i, 1, None, ALU.logical_shift_right),
                     reads=("stats",), writes=("stats",))
            bld.emit("vector", lambda e: e.tensor_scalar(s3i, s3i, -1, None, ALU.bitwise_xor),
                     reads=("stats",), writes=("stats",))
            bld.emit("vector", lambda e: e.tensor_scalar(s3i, s3i, 0x5f3759df + 1, None, ALU.add),
                     reads=("stats",), writes=("stats",))
            for _nr in range(2):
                bld.emit("vector", lambda e: e.tensor_tensor(q_, s3, s3, ALU.mult),
                         reads=("stats",), writes=("stats",))
                bld.emit("vector", lambda e: e.tensor_tensor(q_, q_, s2, ALU.mult),
                         reads=("stats",), writes=("stats",))
                bld.emit("vector", lambda e: e.tensor_scalar(q_, q_, -0.5, 1.5, ALU.mult, ALU.add),
                         reads=("stats",), writes=("stats",))
                bld.emit("vector", lambda e: e.tensor_tensor(s3, s3, q_, ALU.mult),
                         reads=("stats",), writes=("stats",))
            bld.emit("vector", lambda e: e.tensor_tensor(q_, s3, s3, ALU.mult),
                     reads=("stats",), writes=("stats",))
            bld.emit("vector", lambda e: e.tensor_tensor(q_, q_, s2, ALU.mult),
                     reads=("stats",), writes=("stats",))
            bld.emit("vector", lambda e: e.tensor_scalar(q_, q_, -0.5, 1.5, ALU.mult, ALU.add),
                     reads=("stats",), writes=("stats",))
            bld.emit("vector", lambda e: e.tensor_tensor(a_, s3, q_, ALU.mult),
                     reads=("stats",), writes=("stats",))
            bld.emit("vector", lambda e: e.tensor_tensor(s2, m_, a_, ALU.mult),
                     reads=("stats",), writes=("stats",))
            bld.emit("vector", lambda e: e.tensor_scalar(b_, s2, -1.0, None, ALU.mult),
                     reads=("stats",), writes=("stats",))

        def emit_bcast(i):
            bld.emit("tensor", lambda e: e.matmul(ps[0][:], ones_row[:], a_, start=True, stop=True),
                     reads=("stats", "ones_row"), writes=("ps0",))
            bld.emit("tensor", lambda e: e.matmul(ps[1][:], ones_row[:], b_, start=True, stop=True),
                     reads=("stats", "ones_row"), writes=("ps1",))

        def sh_piece(hsc):
            if hsc < 2:
                return f"sh_p{hsc}"
            if hsc < 4:
                return "sh_p2"
            if hsc < 8:
                return "sh_p3"
            return "sh_p4" if hsc < 12 else "sh_p5"

        def ex_piece(ex):
            if ex < 4:
                return "ex_p0" if ex < 2 else "ex_p1"
            return f"ex_scQ{ex // 4}"

        def dn_piece(oc):
            if oc < 2:
                return f"dn_p{oc}"
            return f"dn_scQ{oc // 2}"

        def emit_sh_dma(hsc, direct=False):
            par = hsc % 3
            bld.emit("sync",
                     lambda e, hsc=hsc, par=par: e.dma_start(
                         out=wsh[par][:], in_=sh_sc[hsc * 128:(hsc + 1) * 128, :]),
                     reads=(sh_piece(hsc),), writes=(f"wsh{par}",), dma=f"d_sh{par}")

        def emit_ex_dma(ex, direct=False):
            par = ex % 2
            bld.emit("sync",
                     lambda e, ex=ex, par=par: e.dma_start(
                         out=wexp[par][:], in_=ex_sc[ex * 128:(ex + 1) * 128, :]),
                     reads=(ex_piece(ex),), writes=(f"wexp{par}",), dma=f"d_ex{par}")

        def emit_dn_dma_c(oc):
            par = oc % 2
            bld.emit("scalar",
                     lambda e, oc=oc, par=par: e.dma_start(
                         out=wdn[par][:, 2048:4096],
                         in_=dn_sc[oc * 128:(oc + 1) * 128, 2048:4096]),
                     reads=(dn_piece(oc),), writes=(f"wdnC{par}",), dma=f"d_dnC{par}")

        def emit_dn_dma(oc, direct=False, with_c=True):
            par = oc % 2
            bld.emit("sync",
                     lambda e, oc=oc, par=par: e.dma_start(
                         out=wdn[par][:, 0:2048],
                         in_=dn_sc[oc * 128:(oc + 1) * 128, 0:2048]),
                     reads=(dn_piece(oc),), writes=(f"wdnA{par}",), dma=f"d_dnA{par}")
            if with_c:
                emit_dn_dma_c(oc)
            bld.emit("sync",
                     lambda e, oc=oc, par=par: e.dma_start(
                         out=wdn[par][:, 4096:8192],
                         in_=dn_sc[oc * 128:(oc + 1) * 128, 4096:8192]),
                     reads=(dn_piece(oc),), writes=(f"wdnB{par}",), dma=f"d_dnB{par}")

        def emit_router(i):
            par = i % 2
            bld.emit("tensor",
                     [(lambda e, kc=kc, par=par: e.matmul(
                         ps[6][0:16, 0:512],
                         rw_bf[:, kc * 16:(kc + 1) * 16],
                         x_sb[par][:, kc * T:(kc + 1) * T],
                         start=(kc == 0), stop=(kc == KC - 1)))
                      for kc in range(KC)],
                     reads=tuple(f"x{par}_{k}" for k in range(KC)) + ("rw_bf",),
                     writes=("ps6",))
            bld.emit("vector", lambda e: e.tensor_copy(xr_r[:], ps[6][0:16, 0:512]),
                     reads=("ps6",), writes=("xr_r",))
            bld.emit("tensor",
                     [(lambda e, ts_=ts_: e.transpose(
                         ps[7][:, 128 + 16 * ts_: 144 + 16 * ts_].bitcast(F32R),
                         xr_r[:, ts_ * 128:(ts_ + 1) * 128], ident_r[:]))
                      for ts_ in range(4)],
                     reads=("xr_r", "ident_r"), writes=("ps7",))
            ab_mms = []
            for ts_ in range(4):
                ab_mms.append(lambda e, ts_=ts_: e.matmul(
                    ps[7][:, 2 * ts_: 2 * ts_ + 1], stats[0:1, 512 + ts_ * 128: 512 + (ts_ + 1) * 128],
                    ones_row[0:1, 0:1], start=(ts_ == 0), stop=False))
                ab_mms.append(lambda e, ts_=ts_: e.matmul(
                    ps[7][:, 2 * ts_ + 1: 2 * ts_ + 2], stats[0:1, 1536 + ts_ * 128: 1536 + (ts_ + 1) * 128],
                    ones_row[0:1, 0:1], start=False, stop=(ts_ == 3)))
            bld.emit("tensor", ab_mms, reads=("stats", "ones_row"), writes=("ps7",))
            bld.emit("vector", lambda e: e.tensor_copy(ab8[:], ps[7][:, 0:8]),
                     reads=("ps7",), writes=("ab8",))
            for ts_ in range(4):
                sc_sl = scores4[:, ts_ * 16:(ts_ + 1) * 16]
                bld.emit("vector",
                         lambda e, ts_=ts_, sc_sl=sc_sl: e.tensor_scalar(
                             sc_sl, ps[7][:, 128 + 16 * ts_: 144 + 16 * ts_], ab8[:, 2 * ts_:2 * ts_ + 1],
                             None, ALU.mult),
                         reads=("ps7", "ab8"), writes=(f"scores_{ts_}",))
                bld.emit("vector",
                         lambda e, ts_=ts_, sc_sl=sc_sl: e.scalar_tensor_tensor(
                             sc_sl, rs16[:], ab8[:, 2 * ts_ + 1:2 * ts_ + 2], sc_sl,
                             ALU.mult, ALU.add),
                         reads=("rs16", "ab8", f"scores_{ts_}"), writes=(f"scores_{ts_}",))



            SCORES = tuple(f"scores_{k}" for k in range(4))
            bld.emit("scalar", lambda e: e.activation(scores4[:], scores4[:], AF.Tanh, scale=0.5),
                     reads=SCORES, writes=SCORES)
            bld.emit("vector",
                     lambda e: e.tensor_scalar(scores4[:], scores4[:], 0.5, 0.5, ALU.mult, ALU.add),
                     reads=SCORES, writes=SCORES)

        def emit_transpose(ts_):
            ct_sl = comb_tm[:, ts_ * 16:(ts_ + 1) * 16]
            bld.emit("tensor",
                     lambda e, ct_sl=ct_sl: e.transpose(ps[0][0:16, 128:192].bitcast(BF16), ct_sl, identbf[:]),
                     reads=("comb_tm_all", "identbf"), writes=("ps0",))
            bld.emit("vector",
                     lambda e, ts_=ts_: e.tensor_copy(comb_fm[:, ts_ * 128:(ts_ + 1) * 128],
                                                      ps[0][0:16, 128:192].bitcast(BF16)),
                     reads=("ps0",), writes=("comb_fm",))

        def emit_chain_all(i):
            SC = scores4[:]
            SCv = SC.rearrange("p (s e) -> p s e", s=4)
            SB = sb64[:]
            SBj = SB.rearrange("p (sg j) -> p sg j", j=4)
            MKv = mask64[:].rearrange("p (s e) -> p s e", s=4)
            CUR = sb64
            CURv = CUR[:].rearrange("p (s e) -> p s e", s=4)
            PENv = pen64[:].rearrange("p (s e) -> p s e", s=4)
            gscv = g16a[:].rearrange("p (s g) -> p s g", g=4)

            def b4(t, n):
                return bass.AP(t, 0, [[4, 128], [1, 4], [0, n]])
            SCORES = tuple(f"scores_{k}" for k in range(4))

            def vop(f, extra_r=(), w=("tkk",)):
                bld.emit("vector", f, reads=("tkk", "bias64") + SCORES + extra_r, writes=w)

            vop(lambda e: e.tensor_tensor(SB, SC, bias64[:], ALU.add))
            vop(lambda e: e.tensor_tensor(g16a[:], SBj[:, :, 0], SBj[:, :, 1], ALU.add))
            vop(lambda e: e.tensor_tensor(g16b[:], SBj[:, :, 2], SBj[:, :, 3], ALU.add))
            vop(lambda e: e.tensor_tensor(g16c[:], SBj[:, :, 0], SBj[:, :, 1], ALU.max))
            vop(lambda e: e.tensor_tensor(g16d[:], SBj[:, :, 2], SBj[:, :, 3], ALU.max))
            vop(lambda e: e.tensor_tensor(g16c[:], g16c[:], g16d[:], ALU.add))
            vop(lambda e: e.tensor_tensor(g16a[:], g16a[:], g16b[:], ALU.max))
            vop(lambda e: e.tensor_tensor(g16a[:], g16a[:], g16c[:], ALU.max))
            vop(lambda e: e.reduce_max(r4a[:], gscv, axis=AX))
            vop(lambda e: e.tensor_tensor(g16b[:].rearrange("p (s g) -> p s g", g=4),
                                          gscv, b4(r4a, 4), ALU.is_ge))
            vop(lambda e: e.tensor_scalar(g16b[:], g16b[:], -BIG, None, ALU.mult))
            vop(lambda e: e.tensor_tensor(g16b[:], g16a[:], g16b[:], ALU.add))
            vop(lambda e: e.reduce_max(r4b[:], g16b[:].rearrange("p (s g) -> p s g", g=4), axis=AX))
            vop(lambda e: e.tensor_tensor(g16c[:].rearrange("p (s g) -> p s g", g=4),
                                          gscv, b4(r4b, 4), ALU.is_ge))
            vop(lambda e: e.tensor_copy(mask64[:].rearrange("p (sg j) -> p sg j", j=4),
                                        bass.AP(g16c, 0, [[16, 128], [1, 16], [0, 4]])))
            vop(lambda e: e.tensor_scalar(mask64[:], mask64[:], BIG, -BIG, ALU.mult, ALU.add))
            vop(lambda e: e.tensor_tensor(mask64[:], SB, mask64[:], ALU.add))
            vop(lambda e: e.tensor_copy(CUR[:], mask64[:]))
            for _r in range(3):
                vop(lambda e: e.reduce_max(r4a[:], CURv, axis=AX))
                vop(lambda e: e.tensor_tensor(PENv, CURv, b4(r4a, 16), ALU.is_ge))
                vop(lambda e: e.tensor_scalar(pen64[:], pen64[:], -BIG, None, ALU.mult))
                vop(lambda e: e.tensor_tensor(CUR[:], CUR[:], pen64[:], ALU.add))
            vop(lambda e: e.reduce_max(r4a[:], CURv, axis=AX))
            vop(lambda e: e.tensor_tensor(PENv, MKv, b4(r4a, 16), ALU.is_ge))
            vop(lambda e: e.tensor_tensor(CUR[:], SC, pen64[:], ALU.mult))
            vop(lambda e: e.tensor_reduce(r4a[:], CURv, axis=AX, op=ALU.add))
            vop(lambda e: e.tensor_scalar(r4a[:], r4a[:], 1e-20, None, ALU.add))
            vop(lambda e: e.reciprocal(r4b[:], r4a[:]))
            bld.emit("vector",
                     lambda e: e.tensor_tensor(comb_tm[:].rearrange("p (s e) -> p s e", s=4),
                                               CURv, b4(r4b, 16), ALU.mult),
                     reads=("tkk",) + SCORES, writes=("comb_tm_all",))

        def emit_ln(i):
            par = i % 2
            for kc in range(KC):
                tb = silu_sb[kc % 2]
                bld.emit("vector",
                         lambda e, kc=kc, tb=tb, par=par: e.tensor_tensor(tb[:], xch(x_sb[par], kc), ps[0][:], ALU.mult),
                         reads=(f"x{par}_{kc}", "ps0"), writes=(f"silu{kc % 2}",))
                bld.emit("vector",
                         lambda e, i=i, kc=kc, tb=tb: e.tensor_tensor(xnbf_tok(i, kc), tb[:], ps[1][:], ALU.add),
                         reads=(f"silu{kc % 2}", "ps1"), writes=(f"xnbf{i % 2}_{kc}",))

        def emit_conv_taps(i, kc):
            hv = hcv2[kc % 2]
            bld.emit("vector",
                     lambda e, kc=kc, i=i, hv=hv: e.tensor_scalar(
                         hv[:], xnbf_shift(i, kc, 17),
                         et_sb[:, kc * 21 + 17: kc * 21 + 18], None, ALU.mult),
                     reads=(f"xnbf{i % 2}_{kc}", "et_sb"), writes=(f"hcv{kc % 2}",))
            for j in range(18, 21):
                bld.emit("vector",
                         lambda e, kc=kc, j=j, i=i, hv=hv: e.scalar_tensor_tensor(
                             hv[:], xnbf_shift(i, kc, j),
                             et_sb[:, kc * 21 + j: kc * 21 + j + 1], hv[:],
                             ALU.mult, ALU.add),
                         reads=(f"xnbf{i % 2}_{kc}", "et_sb", f"hcv{kc % 2}"), writes=(f"hcv{kc % 2}",))

        def emit_halo(i):
            p = i % 2
            q = (i - 1) % 2
            bld.emit("scalar",
                     [(lambda e, kc=kc, p=p, q=q: e.copy(
                         xn_bf2[p][:, kc * CW + HALO - TAPS: kc * CW + HALO],
                         xn_bf2[q][:, kc * CW + HALO + T - TAPS: kc * CW + HALO + T]))
                      for kc in range(KC)],
                     reads=XNBF(i - 1), writes=XNBF(i))

        emit_x_load(0)
        cast_pieces = [
            ("sh_p0", sh_sc[0:128, :], shp[0:128, :]),
            ("sh_p1", sh_sc[128:256, :], shp[128:256, :]),
            ("sh_p2", sh_sc[256:512, :], shp[256:512, :]),
            ("sh_p3", sh_sc[512:1024, :], shp[512:1024, :]),
            ("sh_p4", sh_sc[1024:1536, :], shp[1024:1536, :]),
            ("sh_p5", sh_sc[1536:2048, :], shp[1536:2048, :]),
            ("ex_p0", ex_sc[0:256, :], exp_p[0:256, :]),
            ("ex_p1", ex_sc[256:512, :], exp_p[256:512, :]),
            ("ex_scQ1", ex_sc[512:1024, :], exp_p[512:1024, :]),
            ("dn_p0", dn_sc[0:128, :], dnp[0:128, :]),
            ("dn_p1", dn_sc[128:256, :], dnp[128:256, :]),
            ("ex_scQ2", ex_sc[1024:1536, :], exp_p[1024:1536, :]),
            ("dn_scQ1", dn_sc[256:512, :], dnp[256:512, :]),
            ("ex_scQ3", ex_sc[1536:2048, :], exp_p[1536:2048, :]),
            ("dn_scQ2", dn_sc[512:768, :], dnp[512:768, :]),
            ("dn_scQ3", dn_sc[768:1024, :], dnp[768:1024, :]),
        ]
        for ci, (res, dst, src) in enumerate(cast_pieces):
            bld.emit("gpsimd", lambda e, dst=dst, src=src: e.dma_start(out=dst, in_=src),
                     writes=(res,), dma=f"d_c{ci}")
            if res == "ex_p1":
                emit_x_load(1)
        emit_sum_mm(0)
        for kc in range(KC):
            emit_square(0, kc)
            emit_sq_mm(kc)
        emit_stats_math(0)
        emit_bcast(0)
        emit_router(0)
        emit_chain_all(0)
        emit_ln(0)
        emit_conv_taps(0, 0)
        emit_conv_taps(0, 1)

        def emit_tile_prologue(it2):
            emit_diag_build(0)
            emit_diag_build(1)
            emit_diag_build(2)
            emit_diag_build(3)
            emit_sh_dma(0)
            emit_sh_dma(1)
            emit_sh_dma(2)
            emit_ex_dma(0)
            emit_ex_dma(1)

        for it in range(n_tiles):
            t0 = it * T
            par = it % 2
            if it == 0:
                emit_tile_prologue(0)

            for kc in range(KC):
                par_d = kc % 4
                pb = 2 + kc % 4
                bld.emit("tensor",
                         [(lambda e, j=j, kc=kc, par_d=par_d, pb=pb, it=it: e.matmul(
                             ps[pb][:], diag[par_d][:, (j - 1) * 128: j * 128], xnbf_shift(it, kc, j),
                             start=(j == 1), stop=(j == 16)))
                          for j in range(1, 17)],
                         reads=(f"diag{par_d}", f"xnbf{it % 2}_{kc}"), writes=(f"ps{pb}",))
                if kc >= 2:
                    emit_conv_taps(it, kc)
                bld.emit("vector",
                         lambda e, kc=kc, pb=pb, hv=hcv2[kc % 2]: e.tensor_tensor(
                             xch(h_conv, kc), ps[pb][:], hv[:], ALU.add),
                         reads=(f"ps{pb}", f"hcv{kc % 2}"), writes=("h_conv",))
                if kc + 4 < KC:
                    emit_diag_build(kc + 4)
            if 1 <= it < n_tiles - 1:
                emit_x_load(it + 1)

            for hsc in range(16):
                par_s = hsc % 3
                sl = hsc % 2
                pg = (6, 2, 3)[hsc % 3]
                pu = (7, 4, 5)[hsc % 3]
                bld.emit("tensor",
                         [(lambda e, kc=kc, pg=pg, par_s=par_s, it=it: e.matmul(
                             ps[pg][:], wsh[par_s][:, kc * 128:(kc + 1) * 128], xnbf_tok(it, kc),
                             start=(kc == 0), stop=(kc == KC - 1)))
                          for kc in range(KC)],
                         reads=(f"wsh{par_s}",) + XNBF(it), writes=(f"ps{pg}",))
                bld.emit("tensor",
                         [(lambda e, kc=kc, pu=pu, par_s=par_s, it=it: e.matmul(
                             ps[pu][:], wsh[par_s][:, 1024 + kc * 128: 1024 + (kc + 1) * 128], xnbf_tok(it, kc),
                             start=(kc == 0), stop=(kc == KC - 1)))
                          for kc in range(KC)],
                         reads=(f"wsh{par_s}",) + XNBF(it), writes=(f"ps{pu}",))
                bld.emit("scalar", lambda e, pg=pg, sl=sl: e.activation(silu_sb[sl][:], ps[pg][:], AF.Silu),
                         reads=(f"ps{pg}",), writes=(f"silu{sl}",))
                bld.emit("vector", lambda e, pu=pu, sl=sl: e.tensor_copy(u_sb[sl][:], ps[pu][:]),
                         reads=(f"ps{pu}",), writes=(f"usb{sl}",))
                bld.emit("gpsimd",
                         lambda e, hsc=hsc, sl=sl: e.tensor_tensor(
                             h_s[:, hsc * T:(hsc + 1) * T], silu_sb[sl][:], u_sb[sl][:], ALU.mult),
                         reads=(f"silu{sl}", f"usb{sl}"), writes=("h_s",))
                if hsc in (5, 7, 9, 11):
                    emit_transpose((hsc - 5) // 2)
                if hsc + 3 < 16:
                    emit_sh_dma(hsc + 3, direct=(it == 0 and hsc + 3 >= 8))

            for ex in range(NE):
                par_e = ex % 2
                pc = 6 + ex % 2
                bld.emit("tensor",
                         lambda e, ex=ex, pc=pc: e.matmul(ps[pc][:], sel_sb[:, ex * 128:(ex + 1) * 128],
                                                          comb_fm[:, :], start=True, stop=True),
                         reads=("sel_sb", "comb_fm"), writes=(f"ps{pc}",))
                for hc in range(2):
                    pg = 2 + hc
                    pu = 4 + hc
                    bld.emit("tensor",
                             [(lambda e, kc=kc, hc=hc, pg=pg, par_e=par_e, it=it: e.matmul(
                                 ps[pg][:], wexp[par_e][:, kc * 256 + hc * 128: kc * 256 + hc * 128 + 128],
                                 xnbf_tok(it, kc), start=(kc == 0), stop=(kc == KC - 1)))
                              for kc in range(KC)],
                             reads=(f"wexp{par_e}",) + XNBF(it), writes=(f"ps{pg}",))
                    bld.emit("tensor",
                             [(lambda e, kc=kc, hc=hc, pu=pu, par_e=par_e, it=it: e.matmul(
                                 ps[pu][:], wexp[par_e][:, 2048 + kc * 256 + hc * 128: 2048 + kc * 256 + hc * 128 + 128],
                                 xnbf_tok(it, kc), start=(kc == 0), stop=(kc == KC - 1)))
                              for kc in range(KC)],
                             reads=(f"wexp{par_e}",) + XNBF(it), writes=(f"ps{pu}",))
                    bld.emit("scalar", lambda e, pg=pg, hc=hc: e.activation(silu_sb[hc][:], ps[pg][:], AF.Silu),
                             reads=(f"ps{pg}",), writes=(f"silu{hc}",))
                    bld.emit("vector", lambda e, pu=pu, hc=hc: e.tensor_copy(u_sb[hc][:], ps[pu][:]),
                             reads=(f"ps{pu}",), writes=(f"usb{hc}",))
                    bld.emit("gpsimd",
                             lambda e, hc=hc: e.tensor_tensor(u_sb[hc][:], silu_sb[hc][:], u_sb[hc][:], ALU.mult),
                             reads=(f"silu{hc}", f"usb{hc}"), writes=(f"usb{hc}",))
                    bld.emit("vector",
                             lambda e, ex=ex, hc=hc, pc=pc: e.tensor_tensor(
                                 h_w[:, (ex * 2 + hc) * T:(ex * 2 + hc + 1) * T],
                                 u_sb[hc][:], ps[pc][:], ALU.mult),
                             reads=(f"usb{hc}", f"ps{pc}"), writes=(f"hw{ex}",))
                if ex + 2 < NE:
                    emit_ex_dma(ex + 2, direct=(it == 0 and ex + 2 >= 8))
                if ex == 11:
                    emit_dn_dma(0)
                if ex == 12:
                    emit_dn_dma(1)
            if it + 1 < n_tiles:
                emit_tile_prologue(it + 1)

            for oc in range(8):
                po = 2 + oc % 2
                par_o = oc % 2
                w0 = wdn[par_o]
                mms = []
                for kc in range(KC):
                    mms.append(lambda e, kc=kc, po=po, w0=w0, it=it: e.matmul(
                        ps[po][:], w0[:, kc * 128:(kc + 1) * 128],
                        xnbf_tok(it, kc), start=(kc == 0), stop=False))
                for kc in range(KC):
                    mms.append(lambda e, kc=kc, po=po, w0=w0, it=it: e.matmul(
                        ps[po][:], w0[:, 1024 + kc * 128: 1024 + (kc + 1) * 128],
                        xch(h_conv, kc), start=False, stop=False))
                for hsc in range(16):
                    mms.append(lambda e, hsc=hsc, po=po, w0=w0: e.matmul(
                        ps[po][:], w0[:, 2048 + hsc * 128: 2048 + (hsc + 1) * 128],
                        h_s[:, hsc * T:(hsc + 1) * T], start=False, stop=False))
                bld.emit("tensor", mms[:16],
                         reads=(f"wdnA{par_o}", "h_conv") + XNBF(it),
                         writes=(f"ps{po}",))
                bld.emit("tensor", mms[16:],
                         reads=(f"wdnA{par_o}", f"wdnC{par_o}", "h_s"),
                         writes=(f"ps{po}",))
                if oc + 2 < 8:
                    emit_dn_dma_c(oc + 2)
                mmsb = []
                for eh in range(32):
                    mmsb.append(lambda e, eh=eh, po=po, w0=w0: e.matmul(
                        ps[po][:], w0[:, 4096 + eh * 128: 4096 + (eh + 1) * 128],
                        h_w[:, eh * T:(eh + 1) * T], start=False, stop=(eh == 31)))
                bld.emit("tensor", mmsb[:16],
                         reads=(f"wdnB{par_o}",) + tuple(f"hw{ex2}" for ex2 in range(8)),
                         writes=(f"ps{po}",))
                bld.emit("tensor", mmsb[16:],
                         reads=tuple(f"hw{ex2}" for ex2 in range(8, 16)),
                         writes=(f"ps{po}",))
                ostage = out_st[oc % 2]
                oname = f"out_st{oc % 2}"
                etmp = silu_sb[oc % 2]
                ename = f"silu{oc % 2}"
                bld.emit("scalar",
                         lambda e, oc=oc, po=po, etmp=etmp: e.activation(
                             etmp[:], ps[po][:], AF.Identity, bias=bias_col[:, oc:oc + 1]),
                         reads=(f"ps{po}", "bias_col"), writes=(ename,))
                bld.emit("gpsimd",
                         lambda e, oc=oc, ostage=ostage, etmp=etmp, par=par: e.tensor_tensor(
                             ostage[:], etmp[:], xch(x_sb[par], oc), ALU.add),
                         reads=(ename, f"x{par}_{oc}"), writes=(oname,))
                odst = out_fm[oc * 128:(oc + 1) * 128, t0:t0 + T]
                bld.emit("sync",
                         lambda e, odst=odst, ostage=ostage: e.dma_start(out=odst, in_=ostage[:]),
                         reads=(oname,), dma=("d_out0" if oc % 2 == 0 else "d_out1"))
                if oc + 2 < 8:
                    emit_dn_dma(oc + 2, with_c=False)
                if it + 1 < n_tiles:
                    if oc == 0:
                        emit_sum_mm(it + 1)
                        for kc2 in range(KC):
                            emit_square(it + 1, kc2)
                            emit_sq_mm(kc2)
                    elif oc == 1:
                        emit_stats_math(it + 1)
                    elif oc == 2:
                        emit_bcast(it + 1)
                    elif oc == 4:
                        emit_ln(it + 1)
                    elif oc == 5:
                        emit_router(it + 1)
                        emit_halo(it + 1)
                    elif oc == 6:
                        emit_chain_all(it + 1)
                    elif oc == 7:
                        emit_conv_taps(it + 1, 0)
                        emit_conv_taps(it + 1, 1)
        bld.finalize_waits("sync")

        @block.sync
        def _(eng):
            bld.replay("sync", eng)

        @block.tensor
        def _(eng):
            bld.replay("tensor", eng)

        @block.vector
        def _(eng):
            bld.replay("vector", eng)

        @block.scalar
        def _(eng):
            bld.replay("scalar", eng)

        @block.gpsimd
        def _(eng):
            bld.replay("gpsimd", eng)

    return nc


def host_prep(inputs):
    f32 = np.float32
    x = np.asarray(inputs["L0_attn_input_data"], dtype=f32)
    W1 = np.asarray(inputs["L0_attn_W1"], dtype=f32)
    W2 = np.asarray(inputs["L0_attn_W2"], dtype=f32)
    bias = np.asarray(inputs["L0_attn_bias"], dtype=f32)
    et = np.asarray(inputs["L0_attn_embeddings_table"], dtype=f32)
    rw = np.asarray(inputs["L0_ffn_router_w"], dtype=f32)
    cb = np.asarray(inputs["L0_ffn_correction_bias"], dtype=f32)
    gw = np.asarray(inputs["L0_ffn_gate_w"], dtype=f32)
    uw = np.asarray(inputs["L0_ffn_up_w"], dtype=f32)
    dw = np.asarray(inputs["L0_ffn_down_w"], dtype=f32)
    sgw = np.asarray(inputs["L0_ffn_shared_gate_w"], dtype=f32)
    suw = np.asarray(inputs["L0_ffn_shared_up_w"], dtype=f32)
    sdw = np.asarray(inputs["L0_ffn_shared_down_w"], dtype=f32)

    c = np.ascontiguousarray
    sg_l = sgw.reshape(16, 128, 8, 128).transpose(0, 3, 2, 1).reshape(16, 128, 1024)
    su_l = suw.reshape(16, 128, 8, 128).transpose(0, 3, 2, 1).reshape(16, 128, 1024)
    shp = c(np.concatenate([sg_l, su_l], axis=2).reshape(2048, 2048))
    gw_l = gw.reshape(16, 8, 128, 2, 128).transpose(0, 2, 1, 3, 4).reshape(16, 128, 2048)
    uw_l = uw.reshape(16, 8, 128, 2, 128).transpose(0, 2, 1, 3, 4).reshape(16, 128, 2048)
    exp_p = c(np.concatenate([gw_l, uw_l], axis=2).reshape(2048, 4096))
    w1_l = W1.reshape(8, 128, 8, 128).transpose(2, 1, 0, 3).reshape(8, 128, 1024)
    w2_l = W2.reshape(8, 128, 8, 128).transpose(2, 1, 0, 3).reshape(8, 128, 1024)
    sdw_l = sdw.reshape(8, 128, 16, 128).transpose(0, 3, 2, 1).reshape(8, 128, 2048)
    dw_l = dw.reshape(16, 2, 128, 8, 128).transpose(3, 2, 0, 1, 4).reshape(8, 128, 4096)
    dnp = c(np.concatenate([w1_l, w2_l, sdw_l, dw_l], axis=2).reshape(1024, 8192))

    shared = {
        "shp": shp,
        "exp_p": exp_p,
        "dnp": dnp,
        "etp": c(et.T.reshape(8, 128, 21).transpose(1, 0, 2).reshape(128, 168)),
        "rwp": c(rw.T.reshape(8, 128, 16).transpose(1, 0, 2).reshape(128, 128)),
        "biasp": c(bias.reshape(8, 128).T),
        "cbp": c(cb.reshape(1, 16)),
        "identbf": np.eye(128, dtype=ml_dtypes.bfloat16),
    }
    sel = np.zeros((16, 2048), dtype=ml_dtypes.bfloat16)
    for e in range(16):
        sel[e, e * 128:(e + 1) * 128] = 1.0
    shared["selp"] = sel
    per_core = [dict(shared, x_fm=c(x[cidx].T)) for cidx in range(B)]
    return per_core


def kernel(**inputs):
    in_maps = host_prep(inputs)
    nc = build_program(NT)
    res = run_bass_kernel_spmd(nc, in_maps, core_ids=list(range(B)))
    out = np.stack([np.asarray(r["out_fm"], dtype=np.float32).T for r in res.results])
    return np.ascontiguousarray(out)
